# version 02 (final)
# baseline (speedup 1.0000x reference)
import os
import numpy as np
import ml_dtypes
_STRIP = int(os.environ.get("KSTRIP", "0"))
_KTAB = int(os.environ.get("KTAB", "1"))
_KMEGA = int(os.environ.get("KMEGA", "1"))
_KDUMP = int(os.environ.get("KDUMP", "1"))
_KEXP = int(os.environ.get("KEXP", "0"))
import concourse.bass as bass
import concourse.mybir as mybir
from concourse.bass_utils import run_bass_kernel_spmd

F32 = mybir.dt.float32
BF16 = mybir.dt.bfloat16
I32 = mybir.dt.int32
AF = mybir.ActivationFunctionType
ALU = mybir.AluOpType
AX = mybir.AxisListType

NCORE = 8
D = 4096
S = 8192
NH = 32
NKV = 8
G = 4
HD = 128
QKV = 6144
NE = 15
E = 8
R = 16
ER = E * R
EPS = 1e-5
C0 = 12.0
SCALE_QK = HD ** -0.5

TA = 1024
NT = TA // 128
NSLOT = TA // 512


def _alibi_slopes(n):
    import math
    cp2 = 2 ** int(math.floor(math.log2(n)))
    base = 2.0 ** (-8.0 / cp2)
    slopes = base ** np.arange(1, cp2 + 1, dtype=np.float64)
    if cp2 != n:
        extra_base = 2.0 ** (-4.0 / cp2)
        extra = extra_base ** np.arange(1, 1 + 2 * (n - cp2), 2, dtype=np.float64)
        slopes = np.concatenate([slopes, extra])
    return slopes


SLOPES = _alibi_slopes(NH)

ENGINES = ("sync", "tensor", "vector", "scalar", "gpsimd")


class Op:
    __slots__ = ("eng", "fn", "deps", "idx", "dma", "tag", "cum")

    def __init__(self, eng, fn, deps, dma, tag):
        self.eng = eng
        self.fn = fn
        self.deps = [d for d in deps if d is not None]
        self.dma = dma
        self.tag = tag
        self.idx = None
        self.cum = None


class Sched:

    def __init__(self, nc):
        self.nc = nc
        self.ops = {e: [] for e in ENGINES}

    def add(self, eng, fn, deps=(), dma=False, tag=""):
        op = Op(eng, fn, deps, dma, tag)
        op.idx = len(self.ops[eng]) + 1
        prev = self.ops[eng][-1].cum if self.ops[eng] else 0
        op.cum = prev + (16 if dma else 1)
        self.ops[eng].append(op)
        return op

    def barrier(self):
        return {e: (self.ops[e][-1] if self.ops[e] else None) for e in ENGINES}

    def emit(self, block, sems, sim_strict=False):
        def body_for(ename):
            ops = self.ops[ename]

            def body(eng):
                hwm = {e: 0 for e in ENGINES}
                prev_cum = 0
                for op in ops:
                    if sim_strict and op.dma and op.idx > 1:
                        if prev_cum > hwm[ename]:
                            eng.wait_ge(sems[ename], prev_cum)
                            hwm[ename] = prev_cum
                    for dep in op.deps:
                        tgt = dep.cum
                        if tgt > hwm[dep.eng]:
                            eng.wait_ge(sems[dep.eng], tgt)
                            hwm[dep.eng] = tgt
                    inst = op.fn(eng)
                    if inst is None:
                        raise RuntimeError(f"op '{op.tag}' on {ename} returned None")
                    inst.then_inc(sems[ename], 16 if op.dma else 1)
                    prev_cum = op.cum

            return body

        for ename in ENGINES:
            if self.ops[ename]:
                getattr(block, ename)(body_for(ename))



def build_launchA(sim_strict=False):
    nc = bass.Bass("TRN2", target_bir_lowering=False)
    x0 = nc.dram_tensor("x0", [TA, D], F32, kind="ExternalInput").ap()
    wb_t = nc.dram_tensor("wb_t", [D, D], F32, kind="ExternalInput").ap()
    bb = nc.dram_tensor("bb", [D], F32, kind="ExternalInput").ap()
    wr_t = nc.dram_tensor("wr_t", [D, NE], F32, kind="ExternalInput").ap()
    rb = nc.dram_tensor("rb", [NE], F32, kind="ExternalInput").ap()
    a_t = nc.dram_tensor("a_t", [D, ER], F32, kind="ExternalInput").ap()
    b_t = nc.dram_tensor("b_t", [ER, D], F32, kind="ExternalInput").ap()
    emap = nc.dram_tensor("emap", [NE], I32, kind="ExternalInput").ap()
    wqkv = nc.dram_tensor("wqkv", [D, QKV], F32, kind="ExternalInput").ap()

    k_out = nc.dram_tensor("k_out", [NT, NKV, 128, 128], BF16, kind="ExternalOutput").ap()
    v_out = nc.dram_tensor("v_out", [NT, NKV, 128, 128], BF16, kind="ExternalOutput").ap()
    q_out = nc.dram_tensor("q_out", [NSLOT, NH, 128, 512], BF16, kind="ExternalOutput").ap()
    x1_out = nc.dram_tensor("x1_out", [TA, D], F32, kind="ExternalOutput").ap()

    from contextlib import ExitStack
    with ExitStack() as st:
        def sbt(name, shape, dt):
            return st.enter_context(nc.sbuf_tensor(name, shape, dt))

        def pst(name, shape, dt=F32):
            return st.enter_context(nc.psum_tensor(name, shape, dt))

        xt = [sbt(f"xt{i}", [128, D], F32) for i in range(2)]
        stats = sbt("stats", [128, 56], F32)
        xlf = sbt("xlf", [128, 1024], F32)
        xfm = sbt("xfm", [128, 32 * TA], BF16)
        wsl = [sbt(f"wsl{i}", [128, D], F32) for i in range(2)]
        wbf = [sbt(f"wbf{i}", [128, D], BF16) for i in range(2)]
        bt_bf = sbt("bt_bf", [128, D], BF16)
        wr_sb = sbt("wr_sb", [128, 32 * NE], F32)
        bbh = sbt("bbh", [128, 32], F32)
        rb_sb = sbt("rb_sb", [NE, 1], F32)
        emap_i = sbt("emap_i", [NE, 1], I32)
        emap_f = sbt("emap_f", [NE, 1], F32)
        etile = sbt("etile", [NE, 128], F32)
        ex2 = sbt("ex2", [NE, 128], F32)
        id_f = sbt("id_f", [128, 128], F32)
        eps_t = sbt("eps_t", [128, 1], F32)
        zero_t = sbt("zero_t", [128, 1], F32)
        lg_fm = sbt("lg_fm", [NE, 128], F32)
        lg_tm = sbt("lg_tm", [128, NE], F32)
        prb = sbt("prb", [128, NE], F32)
        cnt = sbt("cnt", [128, NE], F32)
        msk = sbt("msk", [128, NE], F32)
        pm = sbt("pm", [128, NE], F32)
        pm_fm = sbt("pm_fm", [NE, 128], F32)
        scr15 = sbt("scr15", [128, NE], F32)
        w_er = sbt("w_er", [128, TA], F32)
        hpr = sbt("hpr", [128, TA], BF16)
        fev = [sbt(f"fev{i}", [128, TA], F32) for i in range(2)]
        x0c = [sbt(f"x0c{i}", [128, TA], F32) for i in range(2)]
        x1c = sbt("x1c", [128, TA], F32)
        qev = [sbt(f"qev{i}", [128, TA], BF16) for i in range(2)]

        pT = [pst(f"pT{i}", [128, 512]) for i in range(2)]
        pMM = [[pst(f"pMM{p}{t}", [128, 512]) for t in range(2)] for p in range(2)]
        pH = [pst(f"pH{i}", [128, 512]) for i in range(2)]

        sems = {e: st.enter_context(nc.semaphore("sem_" + e)) for e in ENGINES}

        sch = Sched(nc)
        A = sch.add

        def fm(ci, col, w):
            return xfm[:, ci * TA + col: ci * TA + col + w]

        def fm_groupview(g4, col):
            r = xfm.rearrange("p (c t) -> p c t", c=32)
            return r[:, g4:g4 + 4, col:col + 128]

        def col_slab(src2d, col0):
            return src2d.rearrange("(c p) o -> p c o", p=128)[:, :, col0:col0 + 128]

        def slab_dst(p):
            return wsl[p][:, :].rearrange("p (c o) -> p c o", c=32)

        d_bbh = A("sync", lambda e: e.dma_start(
            out=bbh[:, :], in_=bb.rearrange("(c p) -> p c", p=128),
            allow_slow_non_contiguous=True), dma=True, tag="d_bbh")
        d_rb = A("sync", lambda e: e.dma_start(out=rb_sb[:, :], in_=rb.unsqueeze(1)), dma=True, tag="d_rb")
        d_em = A("sync", lambda e: e.dma_start(out=emap_i[:, :], in_=emap.unsqueeze(1)), dma=True, tag="d_em")
        d_wr = A("sync", lambda e: e.dma_start(
            out=wr_sb[:, :].rearrange("p (c n) -> p c n", c=32), in_=wr_t.rearrange("(c p) n -> p c n", p=128),
            allow_slow_non_contiguous=True), dma=True, tag="d_wr")
        d_bt = A("sync", lambda e: e.dma_start(
            out=wsl[0][:, :], in_=b_t), dma=True, tag="d_bt")
        d_at = A("sync", lambda e: e.dma_start(
            out=slab_dst(1), in_=a_t.rearrange("(c p) er -> p c er", p=128)), dma=True, tag="d_at")

        o_eps = A("gpsimd", lambda e: e.memset(eps_t[:, :], EPS), tag="eps")
        o_zero = A("gpsimd", lambda e: e.memset(zero_t[:, :], 0.0), tag="zero")
        o_id0 = A("gpsimd", lambda e: e.memset(id_f[:, :], 0.0), tag="id0")
        o_id = A("gpsimd", lambda e: e.affine_select(
            out=id_f[:, :], in_=id_f[:, :], compare_op=ALU.not_equal, fill=1.0,
            base=0, pattern=[[-1, 128]], channel_multiplier=1), deps=[o_id0], tag="id")
        o_et = A("gpsimd", lambda e: e.iota(
            etile[:, :], [[1, E], [0, R]], base=0, channel_multiplier=0,
            allow_small_or_imprecise_dtypes=True), tag="etile")
        o_btb = A("gpsimd", lambda e: e.tensor_copy(bt_bf[:, :], wsl[0][:, :]), deps=[d_bt], tag="btb")
        o_atb = A("gpsimd", lambda e: e.tensor_copy(wbf[1][:, :], wsl[1][:, :]), deps=[d_at], tag="atb")
        o_emf = A("vector", lambda e: e.tensor_copy(emap_f[:, :], emap_i[:, :]), deps=[d_em], tag="emf")
        o_ex2 = A("vector", lambda e: e.tensor_scalar(
            ex2[:, :], etile[:, :], emap_f[:, :], None, ALU.is_equal), deps=[o_et, o_emf], tag="ex2")
        o_bbh = A("scalar", lambda e: e.activation(
            bbh[:, :], bbh[:, :], AF.Copy, bias=0.0, scale=0.5), deps=[d_bbh], tag="bbh")

        prev_norm = [None, None]
        prev_lastT = [None, None]
        prev_pmf = None
        prev_lgt = None
        prev_werev = None
        prev_lg = None
        for i in range(NT):
            p = i % 2
            t = xt[p]
            tcol = i * 128
            d_x = A("sync", lambda e, t=t, i=i: e.dma_start(
                out=t[:, :], in_=x0[i * 128:(i + 1) * 128, :]),
                deps=[prev_norm[p], prev_lastT[p]], dma=True, tag=f"x0_{i}")
            o_bn = None
            for j in range(8):
                o_bn = A("vector", lambda e, t=t, j=j: e.bn_stats(
                    stats[:, 6 * j:6 * j + 6], t[:, 512 * j:512 * (j + 1)]),
                    deps=[d_x] if j == 0 else [], tag=f"bn{i}_{j}")
            o_ag = A("vector", lambda e: e.bn_aggr(stats[:, 48:50], stats[:, 0:48]),
                     deps=[o_bn], tag=f"ag{i}")
            o_ln = A("scalar", lambda e: e.activation(
                stats[:, 50:51], stats[:, 49:50], AF.Ln, bias=eps_t[:, :], scale=1.0),
                deps=[o_ag, o_eps], tag=f"ln{i}")
            o_rstd = A("scalar", lambda e: e.activation(
                stats[:, 51:52], stats[:, 50:51], AF.Exp, bias=zero_t[:, :], scale=-0.5),
                deps=[o_ln, o_zero], tag=f"rstd{i}")
            o_nm = A("vector", lambda e, t=t: e.tensor_scalar(
                t[:, :], t[:, :], stats[:, 48:49], stats[:, 51:52], ALU.subtract, ALU.mult),
                deps=[o_rstd], tag=f"norm{i}")
            prev_norm[p] = o_nm
            ev_b = [None] * 8
            ev_f32 = [None] * 8
            router_mm = [None] * 8
            for g in range(8):
                bank = pT[g % 2]
                deps_first = [o_nm, o_id]
                if g >= 2:
                    deps_first.append(ev_b[g - 2])
                elif i > 0:
                    deps_first.append(prev_pmf if g == 1 else prev_lgt)
                o_tp = None
                for u in range(4):
                    ci = g * 4 + u
                    o_tp = A("tensor", lambda e, bank=bank, u=u, ci=ci, t=t: e.transpose(
                        bank[:, u * 128:(u + 1) * 128], t[:, ci * 128:(ci + 1) * 128], id_f[:, :]),
                        deps=deps_first if u == 0 else [], tag=f"tp{i}_{ci}")
                prev_lastT[p] = o_tp
                xl_half = (g % 2) * 512
                dep_xlf = [router_mm[g - 2]] if g >= 2 else []
                ev_f32[g] = A("scalar", lambda e, bank=bank, xl_half=xl_half: e.activation(
                    xlf[:, xl_half:xl_half + 512], bank[:, :], AF.Copy, bias=0.0, scale=1.0),
                    deps=[o_tp] + dep_xlf, tag=f"evf{i}_{g}")
                ev_b[g] = A("scalar", lambda e, bank=bank, g=g, tcol=tcol: e.activation(
                    fm_groupview(g * 4, tcol), bank[:, :], AF.Copy, bias=0.0, scale=1.0),
                    deps=[o_tp], tag=f"evb{i}_{g}")
                o_mm = None
                for u in range(4):
                    ci = g * 4 + u
                    dd = [ev_f32[g], d_wr]
                    if g == 0 and u == 0 and prev_lg is not None:
                        dd.append(prev_lg)
                    o_mm = A("tensor", lambda e, ci=ci, g=g, u=u, xl_half=xl_half: e.matmul(
                        pH[0][0:NE, 0:128], wr_sb[:, ci * NE:(ci + 1) * NE],
                        xlf[:, xl_half + u * 128: xl_half + (u + 1) * 128],
                        start=(ci == 0), stop=(ci == 31)),
                        deps=dd if u == 0 else [], tag=f"rmm{i}_{ci}")
                router_mm[g] = o_mm
            o_lg = A("scalar", lambda e: e.activation(
                lg_fm[:, :], pH[0][0:NE, 0:128], AF.Identity, bias=rb_sb[:, :], scale=1.0),
                deps=[router_mm[7], d_rb], tag=f"lg{i}")
            prev_lg = o_lg
            o_tp15 = A("tensor", lambda e: e.transpose(
                pT[0][:, 0:NE], lg_fm[:, :], id_f[0:NE, 0:NE]),
                deps=[o_lg, ev_b[6]], tag=f"tp15_{i}")
            o_lgt = A("scalar", lambda e: e.activation(
                lg_tm[:, :], pT[0][:, 0:NE], AF.Copy, bias=0.0, scale=1.0),
                deps=[o_tp15], tag=f"lgt{i}")
            prev_lgt = o_lgt
            o_mx = A("vector", lambda e: e.reduce_max(
                stats[:, 52:53], lg_tm[:, :], axis=AX.X), deps=[o_lgt], tag=f"mx{i}")
            o_ng = A("vector", lambda e: e.tensor_scalar(
                stats[:, 52:53], stats[:, 52:53], -1.0, None, ALU.mult), deps=[o_mx], tag=f"ng{i}")
            o_ex = A("scalar", lambda e: e.activation(
                prb[:, :], lg_tm[:, :], AF.Exp, bias=stats[:, 52:53], scale=1.0,
                accum_out=stats[:, 53:54]), deps=[o_ng], tag=f"sx{i}")
            o_rcp = A("vector", lambda e: e.reciprocal(stats[:, 54:55], stats[:, 53:54]),
                      deps=[o_ex], tag=f"rcp{i}")
            o_cn = None
            for j in range(NE):
                o_cn = A("vector", lambda e, j=j: e.tensor_scalar(
                    scr15[:, :], prb[:, :], prb[:, j:j + 1], 0.0, ALU.is_gt, ALU.add,
                    accum_out=cnt[:, j:j + 1]),
                    deps=[o_ex] if j == 0 else [o_cn], tag=f"cnt{i}_{j}")
            o_mk = A("vector", lambda e: e.tensor_scalar(
                msk[:, :], cnt[:, :], 7.5, None, ALU.is_lt), deps=[o_cn], tag=f"mk{i}")
            o_pm = A("vector", lambda e: e.scalar_tensor_tensor(
                pm[:, :], prb[:, :], stats[:, 54:55], msk[:, :], ALU.mult, ALU.mult),
                deps=[o_mk, o_rcp], tag=f"pm{i}")
            o_tpm = A("tensor", lambda e: e.transpose(
                pT[1][0:NE, 0:128], pm[:, :], id_f[:, :]), deps=[o_pm, ev_b[7]], tag=f"tpm{i}")
            o_pmf = A("scalar", lambda e: e.activation(
                pm_fm[:, :], pT[1][0:NE, 0:128], AF.Copy, bias=0.0, scale=1.0),
                deps=[o_tpm], tag=f"pmf{i}")
            prev_pmf = o_pmf
            dd = [o_pmf, o_ex2]
            if prev_werev is not None:
                dd.append(prev_werev)
            o_wmm = A("tensor", lambda e: e.matmul(
                pH[1][:, 0:128], ex2[:, :], pm_fm[:, :], start=True, stop=True),
                deps=dd, tag=f"wmm{i}")
            o_werev = A("scalar", lambda e, tcol=tcol: e.activation(
                w_er[:, tcol:tcol + 128], pH[1][:, 0:128], AF.Copy, bias=0.0, scale=2.0),
                deps=[o_wmm], tag=f"werev{i}")
            prev_werev = o_werev

        bar1 = sch.barrier()

        o_h = []
        for th in range(2):
            o_mm = None
            for ci in range(32):
                o_mm = A("tensor", lambda e, ci=ci, th=th: e.matmul(
                    pH[th][:, :], wbf[1][:, ci * 128:(ci + 1) * 128], fm(ci, th * 512, 512),
                    start=(ci == 0), stop=(ci == 31)),
                    deps=[bar1["scalar"], o_atb] if ci == 0 else [], tag=f"hmm{th}_{ci}")
            o_h.append(o_mm)
        o_hp = []
        for th in range(2):
            o_hp.append(A("vector", lambda e, th=th: e.tensor_tensor(
                hpr[:, th * 512:(th + 1) * 512], pH[th][:, :], w_er[:, th * 512:(th + 1) * 512],
                ALU.mult), deps=[o_h[th], bar1["scalar"]], tag=f"hp{th}"))

        prev_cast = [o_btb, o_atb]
        prev_lastmm = [None, o_h[1]]
        prev_evac = [None, None]
        prev_add = [None, None]
        prev_x0dma = [None, None]
        prev_x1dma = [None, None]

        def stream_w(idx, src_ap, tag):
            p = idx % 2
            d = A("sync", lambda e, p=p, ap=src_ap: e.dma_start(out=slab_dst(p), in_=ap),
                  deps=[prev_cast[p]], dma=True, tag=f"wd_{tag}")
            c = A("gpsimd", lambda e, p=p: e.tensor_copy(wbf[p][:, :], wsl[p][:, :]),
                  deps=[d, prev_lastmm[p]], tag=f"wc_{tag}")
            prev_cast[p] = c
            return c

        for oc in range(32):
            p = oc % 2
            c = stream_w(oc, col_slab(wb_t, oc * 128), f"b{oc}")
            o_last = None
            for th in range(2):
                dd = [c, o_hp[th], prev_evac[p]]
                o_mm = None
                for ci in range(32):
                    o_mm = A("tensor", lambda e, p=p, th=th, ci=ci: e.matmul(
                        pMM[p][th][:, :], wbf[p][:, ci * 128:(ci + 1) * 128], fm(ci, th * 512, 512),
                        start=(ci == 0), stop=False), deps=dd if ci == 0 else [],
                        tag=f"bmm{oc}_{th}_{ci}")
                o_last = A("tensor", lambda e, p=p, th=th, oc=oc: e.matmul(
                    pMM[p][th][:, :], bt_bf[:, oc * 128:(oc + 1) * 128],
                    hpr[:, th * 512:(th + 1) * 512], start=False, stop=True),
                    deps=[o_btb] if oc == 0 and th == 0 else [], tag=f"lora{oc}_{th}")
            prev_lastmm[p] = o_last
            o_ev = []
            for th in range(2):
                o_ev.append(A("scalar", lambda e, p=p, th=th, oc=oc: e.activation(
                    fev[p][:, th * 512:(th + 1) * 512], pMM[p][th][:, :], AF.Identity,
                    bias=bbh[:, oc:oc + 1], scale=0.5),
                    deps=[o_last, o_bbh], tag=f"fevac{oc}_{th}"))
            prev_evac[p] = o_ev[1]
            d_x0 = A("sync", lambda e, p=p, oc=oc: e.dma_start(
                out=x0c[p][:, :].rearrange("q (tt o) -> q tt o", tt=8),
                in_=x0.rearrange("(tt q) o -> q tt o", q=128)[:, :, oc * 128:(oc + 1) * 128]),
                deps=[prev_x0dma[p]], dma=True, tag=f"x0c{oc}")
            prev_x0dma[p] = d_x0
            o_addh = [None, None]
            for half in range(2):
                bank = pT[half]
                o_tp = None
                for u in range(4):
                    tt = half * 4 + u
                    dd = []
                    if u == 0:
                        dd = [o_ev[0] if half == 0 else o_ev[1], prev_add[half]]
                        if oc == 0:
                            dd.append(bar1["vector"])
                    o_tp = A("tensor", lambda e, bank=bank, u=u, tt=tt, p=p: e.transpose(
                        bank[:, u * 128:(u + 1) * 128], fev[p][:, tt * 128:(tt + 1) * 128],
                        id_f[:, :]), deps=dd, tag=f"ftp{oc}_{tt}")
                o_addh[half] = A("vector", lambda e, half=half, p=p: e.tensor_tensor(
                    x1c[:, half * 512:(half + 1) * 512], pT[half][:, :],
                    x0c[p][:, half * 512:(half + 1) * 512], ALU.add),
                    deps=[o_tp, d_x0, prev_x1dma[p]], tag=f"x1add{oc}_{half}")
                prev_add[half] = o_addh[half]
            prev_x1dma[p] = A("sync", lambda e, oc=oc: e.dma_start(
                out=x1_out.rearrange("(tt q) o -> q tt o", q=128)[:, :, oc * 128:(oc + 1) * 128],
                in_=x1c[:, :].rearrange("q (tt o) -> q tt o", tt=8)),
                deps=[o_addh[0], o_addh[1]], dma=True, tag=f"x1dma{oc}")

        bar2 = sch.barrier()

        prev_norm2 = [None, None]
        prev_lastT2 = [None, None]
        prev_evb2 = [bar2["scalar"]] * 2
        for i in range(NT):
            p = i % 2
            t = xt[p]
            tcol = i * 128
            d_x = A("sync", lambda e, t=t, i=i: e.dma_start(
                out=t[:, :], in_=x1_out[i * 128:(i + 1) * 128, :]),
                deps=[prev_norm2[p], prev_lastT2[p], bar2["sync"], prev_x1dma[0], prev_x1dma[1]],
                dma=True, tag=f"x1rd{i}")
            o_bn = None
            for j in range(8):
                o_bn = A("vector", lambda e, t=t, j=j: e.bn_stats(
                    stats[:, 6 * j:6 * j + 6], t[:, 512 * j:512 * (j + 1)]),
                    deps=[d_x, bar2["scalar"]] if j == 0 else [], tag=f"bn2_{i}_{j}")
            o_ag = A("vector", lambda e: e.bn_aggr(stats[:, 48:50], stats[:, 0:48]),
                     deps=[o_bn], tag=f"ag2_{i}")
            o_ln = A("scalar", lambda e: e.activation(
                stats[:, 50:51], stats[:, 49:50], AF.Ln, bias=eps_t[:, :], scale=1.0),
                deps=[o_ag], tag=f"ln2_{i}")
            o_rstd = A("scalar", lambda e: e.activation(
                stats[:, 51:52], stats[:, 50:51], AF.Exp, bias=zero_t[:, :], scale=-0.5),
                deps=[o_ln], tag=f"rstd2_{i}")
            o_nm = A("vector", lambda e, t=t: e.tensor_scalar(
                t[:, :], t[:, :], stats[:, 48:49], stats[:, 51:52], ALU.subtract, ALU.mult),
                deps=[o_rstd], tag=f"n2_{i}")
            prev_norm2[p] = o_nm
            for g in range(8):
                bank = pT[g % 2]
                o_tp = None
                for u in range(4):
                    ci = g * 4 + u
                    dd = []
                    if u == 0:
                        dd = [o_nm, prev_evb2[g % 2]]
                        if i == 0 and g < 2:
                            dd.append(bar2["vector"])
                    o_tp = A("tensor", lambda e, bank=bank, u=u, ci=ci, t=t: e.transpose(
                        bank[:, u * 128:(u + 1) * 128], t[:, ci * 128:(ci + 1) * 128], id_f[:, :]),
                        deps=dd, tag=f"t2p{i}_{ci}")
                prev_lastT2[p] = o_tp
                o_eb = A("scalar", lambda e, bank=bank, g=g, tcol=tcol: e.activation(
                    fm_groupview(g * 4, tcol), bank[:, :], AF.Copy, bias=0.0, scale=1.0),
                    deps=[o_tp], tag=f"e2b{i}_{g}")
                prev_evb2[g % 2] = o_eb

        bar3 = sch.barrier()

        prev_qdma = [None, None]
        prev_vkt = None
        prev_evT = [bar3["scalar"]] * 2
        for qo in range(48):
            p = qo % 2
            c = stream_w(32 + qo, col_slab(wqkv, qo * 128), f"q{qo}")
            o_lastmm = None
            for th in range(2):
                dd = [c, prev_evac[p]]
                if qo == 0 and th == 0:
                    dd.append(bar3["scalar"])
                o_mm = None
                for ci in range(32):
                    o_mm = A("tensor", lambda e, p=p, th=th, ci=ci: e.matmul(
                        pMM[p][th][:, :], wbf[p][:, ci * 128:(ci + 1) * 128], fm(ci, th * 512, 512),
                        start=(ci == 0), stop=(ci == 31)), deps=dd if ci == 0 else [],
                        tag=f"qmm{qo}_{th}_{ci}")
                o_lastmm = o_mm
            prev_lastmm[p] = o_lastmm
            if qo < 40:
                sc = SCALE_QK if qo < 32 else 1.0
                o_qe = []
                for th in range(2):
                    o_qe.append(A("scalar", lambda e, p=p, th=th, sc=sc: e.activation(
                        qev[p][:, th * 512:(th + 1) * 512], pMM[p][th][:, :], AF.Copy,
                        bias=0.0, scale=sc), deps=[o_lastmm, prev_qdma[p]], tag=f"qev{qo}_{th}"))
                prev_evac[p] = o_qe[1]
                if qo < 32:
                    dl = None
                    for s_ in range(NSLOT):
                        dl = A("sync", lambda e, p=p, qo=qo, s_=s_: e.dma_start(
                            out=q_out[s_, qo], in_=qev[p][:, s_ * 512:(s_ + 1) * 512]),
                            deps=[o_qe[s_]], dma=True, tag=f"qdma{qo}_{s_}")
                    prev_qdma[p] = dl
                else:
                    h = qo - 32
                    dl = None
                    for kt in range(NT):
                        dl = A("sync", lambda e, p=p, h=h, kt=kt: e.dma_start(
                            out=k_out[kt, h], in_=qev[p][:, kt * 128:(kt + 1) * 128]),
                            deps=[o_qe[kt // 4]], dma=True, tag=f"kdma{h}_{kt}")
                    prev_qdma[p] = dl
            else:
                h = qo - 40
                o_ve = []
                for th in range(2):
                    o_ve.append(A("scalar", lambda e, p=p, th=th: e.activation(
                        fev[p][:, th * 512:(th + 1) * 512], pMM[p][th][:, :], AF.Copy,
                        bias=0.0, scale=1.0), deps=[o_lastmm, prev_evac[p]], tag=f"vev{qo}_{th}"))
                prev_evac[p] = o_ve[1]
                for half in range(2):
                    bank = pT[half]
                    o_tp = None
                    for u in range(4):
                        tt = half * 4 + u
                        dd = []
                        if u == 0:
                            dd = [o_ve[half], prev_evT[half]]
                        o_tp = A("tensor", lambda e, bank=bank, u=u, tt=tt, p=p: e.transpose(
                            bank[:, u * 128:(u + 1) * 128], fev[p][:, tt * 128:(tt + 1) * 128],
                            id_f[:, :]), deps=dd, tag=f"vtp{h}_{tt}")
                    o_et_ = A("scalar", lambda e, bank=bank, p=p, half=half: e.activation(
                        qev[p][:, half * 512:(half + 1) * 512], bank[:, :], AF.Copy,
                        bias=0.0, scale=1.0), deps=[o_tp, prev_qdma[p]], tag=f"vet{h}_{half}")
                    prev_evT[half] = o_et_
                    dl = None
                    for u in range(4):
                        tt = half * 4 + u
                        dl = A("sync", lambda e, p=p, half=half, u=u, tt=tt, h=h: e.dma_start(
                            out=v_out[tt, h], in_=qev[p][:, (half * 4 + u) * 128:(half * 4 + u + 1) * 128]),
                            deps=[o_et_], dma=True, tag=f"vdma{h}_{tt}")
                    prev_qdma[p] = dl

        sch.emit(st.enter_context(nc.Block()), sems, sim_strict=sim_strict)
    return nc



NKT = S // 128
TW = 16896


def build_launchB(sim_strict=False, attn_only=False, npass=16, nslot_b=NSLOT):
    nc = bass.Bass("TRN2", target_bir_lowering=False,
                   detect_race_conditions=not sim_strict)
    x1 = nc.dram_tensor("x1", [TA, D], F32, kind="ExternalInput").ap()
    q_in = nc.dram_tensor("q_in", [NSLOT, NH, 128, 512], BF16, kind="ExternalInput").ap()
    kg = nc.dram_tensor("kg", [NKT, NKV, 128, 128], BF16, kind="ExternalInput").ap()
    vg = nc.dram_tensor("vg", [NKT, NKV, 128, 128], BF16, kind="ExternalInput").ap()
    wo = nc.dram_tensor("wo", [D, D], F32, kind="ExternalInput").ap()
    wb_t = nc.dram_tensor("wb_t", [D, D], F32, kind="ExternalInput").ap()
    bb = nc.dram_tensor("bb", [D], F32, kind="ExternalInput").ap()
    wr_t = nc.dram_tensor("wr_t", [D, NE], F32, kind="ExternalInput").ap()
    rb = nc.dram_tensor("rb", [NE], F32, kind="ExternalInput").ap()
    a_t = nc.dram_tensor("a_t", [D, ER], F32, kind="ExternalInput").ap()
    b_t = nc.dram_tensor("b_t", [ER, D], F32, kind="ExternalInput").ap()
    emap = nc.dram_tensor("emap", [NE], I32, kind="ExternalInput").ap()
    scal = nc.dram_tensor("scal", [8], I32, kind="ExternalInput").ap()

    y = nc.dram_tensor("y", [TA, D], F32, kind="ExternalOutput").ap()
    NW = TW // 128
    t2r = nc.dram_tensor("t2r", [128, NW, 512], F32).ap()
    x2d = nc.dram_tensor("x2d", [TA, D], F32).ap()

    from contextlib import ExitStack
    with ExitStack() as st:
        def sbt(name, shape, dt):
            return st.enter_context(nc.sbuf_tensor(name, shape, dt))

        def pst(name, shape, dt=F32):
            return st.enter_context(nc.psum_tensor(name, shape, dt))

        q_sb = sbt("q_sb", [128, NH * 512], BF16)
        big_fm = sbt("big_fm", [128, 32 * 512], BF16)
        k_sb = sbt("k_sb", [128, 2 * 4 * 128], BF16)
        v_sb = sbt("v_sb", [128, 2 * 4 * 128], BF16)
        d_sb = sbt("d_sb", [128, 2 * 4 * 512], F32)
        p_sb = sbt("p_sb", [128, 8 * 512], BF16)
        linv = sbt("linv", [1, 2 * 512], F32)
        lbc = sbt("lbc", [128, 512], F32)
        t2b = sbt("t2b", [128, 1536], F32)
        xt = [sbt("xtB", [128, D], F32)]
        stats = sbt("stats", [128, 56], F32)
        xlf = sbt("xlf", [128, 512], F32)
        wsl = [sbt(f"wsl{i}", [128, D], F32) for i in range(2)]
        wbf = [sbt(f"wbf{i}", [128, D], BF16) for i in range(2)]
        bt_bf = sbt("bt_bf", [128, D], BF16)
        at_bf = sbt("at_bf", [128, ER * 32], BF16)
        wr_sb = sbt("wr_sb", [128, 32 * NE], F32)
        bbh = sbt("bbh", [128, 32], F32)
        rb_sb = sbt("rb_sb", [NE, 1], F32)
        emap_i = sbt("emap_i", [NE, 1], I32)
        emap_f = sbt("emap_f", [NE, 1], F32)
        etile = sbt("etile", [NE, 128], F32)
        ex2 = sbt("ex2", [NE, 128], F32)
        id_f = sbt("id_f", [128, 128], F32)
        eps_t = sbt("eps_t", [128, 1], F32)
        zero_t = sbt("zero_t", [128, 1], F32)
        zeros_bf = sbt("zeros_bf", [128, 128], BF16)
        ones_bf = sbt("ones_bf", [128, 1], BF16)
        ones_f = sbt("ones_f", [1, 128], F32)
        lg_fm = sbt("lg_fm", [NE, 128], F32)
        lg_tm = sbt("lg_tm", [128, NE], F32)
        prb = sbt("prb", [128, NE], F32)
        cnt = sbt("cnt", [128, NE], F32)
        msk = sbt("msk", [128, NE], F32)
        pm = sbt("pm", [128, NE], F32)
        pm_fm = sbt("pm_fm", [NE, 128], F32)
        scr15 = sbt("scr15", [128, NE], F32)
        w_er = sbt("w_er", [128, 512], F32)
        hpr = sbt("hpr", [128, 512], BF16)
        fev = [sbt(f"fev{i}", [128, 512], F32) for i in range(2)]
        xc = [sbt(f"xc{i}", [128, 512], F32) for i in range(2)]
        x2c = sbt("x2c", [128, 512], F32)
        scal_sb = sbt("scal_sb", [1, 8], I32)
        junk = sbt("junk", [1, 8], I32)

        pS = [pst(f"pS{i}", [128, 512]) for i in range(4)]
        pO = [pst(f"pO{i}", [128, 512]) for i in range(2)]
        pL = [pst(f"pL{i}", [128, 512]) for i in range(2)]

        sems = {e: st.enter_context(nc.semaphore("sem_" + e)) for e in ENGINES}
        lt = st.enter_context(nc.semaphore("lt"))
        lv = st.enter_context(nc.semaphore("lv"))
        ls = st.enter_context(nc.semaphore("ls"))
        ld = st.enter_context(nc.semaphore("ld"))
        le = st.enter_context(nc.semaphore("le"))

        sch = Sched(nc)
        A = sch.add
        GPD = "sync" if sim_strict else "gpsimd"

        d_bbh = A("sync", lambda e: e.dma_start(
            out=bbh[:, :], in_=bb.rearrange("(c p) -> p c", p=128),
            allow_slow_non_contiguous=True), dma=True, tag="d_bbh")
        d_rb = A("sync", lambda e: e.dma_start(out=rb_sb[:, :], in_=rb.unsqueeze(1)), dma=True)
        d_em = A("sync", lambda e: e.dma_start(out=emap_i[:, :], in_=emap.unsqueeze(1)), dma=True)
        d_wr = A("sync", lambda e: e.dma_start(
            out=wr_sb[:, :].rearrange("p (c n) -> p c n", c=32),
            in_=wr_t.rearrange("(c p) n -> p c n", p=128),
            allow_slow_non_contiguous=True), dma=True)
        d_bt = A("sync", lambda e: e.dma_start(out=wsl[0][:, :], in_=b_t), dma=True)
        d_at = A("sync", lambda e: e.dma_start(
            out=wsl[1][:, :].rearrange("p (c o) -> p c o", c=32),
            in_=a_t.rearrange("(c p) er -> p c er", p=128)), dma=True)
        d_sc = A("sync", lambda e: e.dma_start(out=scal_sb[:, :], in_=scal.unsqueeze(0)), dma=True)

        o_eps = A("gpsimd", lambda e: e.memset(eps_t[:, :], EPS))
        o_zero = A("gpsimd", lambda e: e.memset(zero_t[:, :], 0.0))
        o_zbf = A("gpsimd", lambda e: e.memset(zeros_bf[:, :], 0.0))
        o_obf = A("gpsimd", lambda e: e.memset(ones_bf[:, :], 1.0))
        o_onf = A("gpsimd", lambda e: e.memset(ones_f[:, :], 1.0))
        o_id0 = A("gpsimd", lambda e: e.memset(id_f[:, :], 0.0))
        o_id = A("gpsimd", lambda e: e.affine_select(
            out=id_f[:, :], in_=id_f[:, :], compare_op=ALU.not_equal, fill=1.0,
            base=0, pattern=[[-1, 128]], channel_multiplier=1), deps=[o_id0])
        o_et = A("gpsimd", lambda e: e.iota(
            etile[:, :], [[1, E], [0, R]], base=0, channel_multiplier=0,
            allow_small_or_imprecise_dtypes=True))
        o_pz = A("gpsimd", lambda e: e.memset(p_sb[:, :], 0.0), tag="pz")
        o_bz = A("gpsimd", lambda e: e.memset(big_fm[:, :], 0.0), tag="bz")
        o_btb = A("gpsimd", lambda e: e.tensor_copy(bt_bf[:, :], wsl[0][:, :]), deps=[d_bt])
        o_atb = A("gpsimd", lambda e: e.tensor_copy(at_bf[:, :], wsl[1][:, :]), deps=[d_at])
        o_emf = A("vector", lambda e: e.tensor_copy(emap_f[:, :], emap_i[:, :]), deps=[d_em])
        o_ex2 = A("vector", lambda e: e.tensor_scalar(
            ex2[:, :], etile[:, :], emap_f[:, :], None, ALU.is_equal), deps=[o_et, o_emf])
        o_bbh = A("scalar", lambda e: e.activation(
            bbh[:, :], bbh[:, :], AF.Copy, bias=0.0, scale=0.5), deps=[d_bbh])

        prev_tq = None
        for j in range(17 if _KTAB else 0):
            nwj = min(8, NW - 8 * j)
            base = -(TW // 2) + 1024 * j
            o_io = A("gpsimd", lambda e, base=base: e.iota(
                t2b[:, :], [[1, 1536]], base=base, channel_multiplier=-1,
                allow_small_or_imprecise_dtypes=True), deps=[prev_tq], tag=f"t2io{j}")
            o_cl = A("gpsimd", lambda e: e.tensor_scalar(
                t2b[:, :], t2b[:, :], 1e-15, None, ALU.max), deps=[o_io], tag=f"t2cl{j}")
            o_l = A("scalar", lambda e: e.activation(
                t2b[:, :], t2b[:, :], AF.Ln, bias=eps_t[:, :], scale=1.0),
                deps=[o_cl, o_eps], tag=f"t2ln{j}")
            o_s = A("scalar", lambda e: e.activation(
                t2b[:, :], t2b[:, :], AF.Exp, bias=zero_t[:, :], scale=0.5),
                deps=[o_l, o_zero], tag=f"t2ex{j}")
            o_m = A("gpsimd", lambda e, base=base: e.affine_select(
                out=t2b[:, :], in_=t2b[:, :], compare_op=ALU.is_ge, fill=1e9,
                base=base, pattern=[[1, 1536]], channel_multiplier=-1),
                deps=[o_s], tag=f"t2ms{j}")
            for m in range(nwj):
                prev_tq = A("sync", lambda e, j=j, m=m: e.dma_start(
                    out=t2r[:, 8 * j + m, :], in_=t2b[:, 128 * m:128 * m + 512]),
                    deps=[o_m], dma=True, tag=f"t2d{j}_{m}")
        o_t2done = prev_tq if prev_tq is not None else d_sc
        setup_bar = sch.barrier()

        NPASS_B = npass
        kg6 = kg.rearrange("k h p t -> (h p) k t")
        vg6 = vg.rearrange("k h p t -> (h p) k t")
        _regpool = {}

        def getregs(eng, ename, names):
            pool = _regpool.setdefault(ename, {})
            for nm in names:
                if nm not in pool:
                    pool[nm] = eng.alloc_register("sh_" + ename + "_" + nm)
            return [pool[nm] for nm in names]

        def wge(eng, sem_, reg, maxv=4000000):
            rv = eng.snap(reg, min_val=0, max_val=maxv)
            eng.wait_ge(sem_, rv)
            eng.free_register(rv.val)

        prev_pass = dict(setup_bar)

        def f_mega(eng):
            if not _KMEGA:
                return eng.dma_start(out=junk[:, :], in_=scal.unsqueeze(0))
            (nb, n, hrow, tgl, wreg, wstart, ktc, wlt, wlv, tmp) = getregs(
                eng, "sync", ["nb", "n", "hrow", "tgl", "wreg", "wstart",
                              "ktc", "wlt", "wlv", "tmp"])
            eng.reg_mov(wlt, 0)
            eng.reg_mov(wlv, 0)
            for s in range(nslot_b):
                eng.reg_load(nb, scal_sb[0:1, s:s + 1])
                eng.reg_load(wstart, scal_sb[0:1, 2 + s:3 + s])
                wge(eng, lt, wlt)
                eng.dma_start(out=q_sb[:, :].rearrange("p (h w) -> p h w", h=NH),
                              in_=q_in[s].rearrange("h p w -> p h w")).then_inc(ld, 16)
                eng.reg_mov(hrow, 0)
                eng.reg_mov(tgl, 0)
                eng.reg_mov(n, NPASS_B)
                with eng.While(n):
                    eng.reg_add(wlt, wlt, 4)
                    eng.reg_add(tmp, nb, 0)
                    eng.reg_add(wreg, wstart, 0)
                    eng.reg_mov(ktc, 0)
                    with eng.While(tmp):
                        wge(eng, lt, wlt)
                        hv = eng.snap(hrow, min_val=0, max_val=(NKV - 1) * 128)
                        kt0 = eng.snap(ktc, min_val=0, max_val=NKT - 8)
                        eng.dma_start(
                            out=k_sb[:, :].rearrange("p (b t) -> p b t", b=8),
                            in_=kg6[bass.ds(hv, 128), bass.ds(kt0, 8), :]).then_inc(ld, 16)
                        eng.dma_start(
                            out=v_sb[:, :].rearrange("p (b t) -> p b t", b=8),
                            in_=vg6[bass.ds(hv, 128), bass.ds(kt0, 8), :]).then_inc(ld, 16)
                        wge(eng, lv, wlv)
                        wv = eng.snap(wreg, min_val=0, max_val=NW - 8)
                        eng.dma_start(
                            out=d_sb[:, :].rearrange("p (m q) -> p m q", m=8),
                            in_=t2r[:, bass.ds(wv, 8), :]).then_inc(ld, 16)
                        eng.reg_sub(wreg, wreg, 8)
                        eng.reg_add(ktc, ktc, 8)
                        eng.reg_add(wlt, wlt, 48)
                        eng.reg_add(wlv, wlv, 16)
                        eng.reg_sub(tmp, tmp, 1)
                    eng.reg_add(wlt, wlt, 4)
                    eng.reg_add(hrow, hrow, tgl)
                    eng.reg_sub(tgl, 128, tgl)
                    eng.reg_sub(n, n, 1)
            return eng.dma_start(out=junk[:, :], in_=scal.unsqueeze(0))

        o_mega = A("sync", f_mega, deps=[o_t2done, setup_bar["sync"], d_sc],
                   dma=True, tag="mega")

        pass_index = [0]

        def attn_pass(s, h, gp, extra):
            heads = [h * G + gp * 2 + g for g in range(2)]
            slopes = [float(SLOPES[hh]) for hh in heads]
            slot_start = (pass_index[0] % 16 == 0)
            pass_index[0] += 1

            o_clear = A("gpsimd", lambda e: e.memset(scr15[0:1, 0:1], 0.0),
                        deps=[prev_pass[e2] for e2 in ("tensor", "vector", "scalar", "gpsimd")]
                        + extra + [d_sc], tag=f"clr{s}_{h}_{gp}")

            def f_tens(eng):
                if _STRIP >= 6:
                    return eng.matmul(pS[0][0:1, 0:1], zeros_bf[0:1, 0:1],
                                      p_sb[0:1, 0:1], start=True, stop=True)
                (n, nb0, bls, bld, tmp, wd, v0, v1, v2, v3, x0r, x1r, x2r, x3r) = getregs(
                    eng, "tensor", ["n", "nb0", "bls", "bld", "tmp", "wd",
                                    "v0", "v1", "v2", "v3", "x0r", "x1r", "x2r", "x3r"])
                vs = (v0, v1, v2, v3)
                xs = (x0r, x1r, x2r, x3r)
                if s == 0 and h == 0 and gp == 0:
                    eng.reg_mov(bls, 0)
                    eng.reg_mov(bld, 0)
                if slot_start:
                    eng.reg_add(bld, bld, 16)
                eng.reg_load(n, scal_sb[0:1, s:s + 1])
                eng.reg_add(nb0, n, 0)
                eng.reg_add(wd, bld, 48)
                for jj in range(4):
                    eng.reg_add(vs[jj], bls, 4 * jj)
                    eng.reg_add(xs[jj], bls, 4 * jj + 4)
                for g in range(2):
                    eng.matmul(pO[g][:, :], zeros_bf[:, :], p_sb[:, 0:512],
                               start=True, stop=False).then_inc(lt, 1)
                    eng.matmul(pL[g][0:1, :], zeros_bf[:, 0:1], p_sb[:, 0:512],
                               start=True, stop=False).then_inc(lt, 1)
                def close_groups():
                    for g in range(2):
                        eng.matmul(pO[g][:, :], zeros_bf[:, :], p_sb[:, 0:512],
                                   start=False, stop=True).then_inc(lt, 1)
                        eng.matmul(pL[g][0:1, :], zeros_bf[:, 0:1], p_sb[:, 0:512],
                                   start=False, stop=True).then_inc(lt, 1)

                if _STRIP >= 5:
                    close_groups()
                    return eng.matmul(pS[0][0:1, 0:1], zeros_bf[0:1, 0:1],
                                      p_sb[0:1, 0:1], start=True, stop=True)
                def qk_section(jj):
                    for j in (2 * jj, 2 * jj + 1):
                        for g in range(2):
                            if _STRIP >= 3:
                                eng.matmul(pS[3][0:1, 0:1], zeros_bf[0:1, 0:1],
                                           p_sb[0:1, 0:1], start=True, stop=True).then_inc(lt, 1)
                            else:
                                eng.matmul(
                                    pS[(2 * j + g) % 4][:, :],
                                    k_sb[:, j * 128:(j + 1) * 128],
                                    q_sb[:, heads[g] * 512:(heads[g] + 1) * 512],
                                    start=True, stop=True).then_inc(lt, 1)

                def av_section(jj):
                    for j in (2 * jj, 2 * jj + 1):
                        for g in range(2):
                            sl = (j % 4) * 2 + g
                            if _STRIP >= 3:
                                eng.matmul(pS[3][0:1, 0:1], zeros_bf[0:1, 0:1],
                                           p_sb[0:1, 0:1], start=True, stop=True).then_inc(lt, 1)
                                eng.matmul(pS[3][0:1, 0:1], zeros_bf[0:1, 0:1],
                                           p_sb[0:1, 0:1], start=True, stop=True).then_inc(lt, 1)
                            else:
                                eng.matmul(
                                    pO[g][:, :], v_sb[:, j * 128:(j + 1) * 128],
                                    p_sb[:, sl * 512:(sl + 1) * 512],
                                    start=False, stop=False).then_inc(lt, 1)
                                eng.matmul(
                                    pL[g][0:1, :], ones_bf[:, :],
                                    p_sb[:, sl * 512:(sl + 1) * 512],
                                    start=False, stop=False).then_inc(lt, 1)

                with eng.While(n):
                    wge(eng, ld, wd)
                    for jj in range(4):
                        wge(eng, ls, vs[jj])
                        qk_section(jj)
                        if jj >= 1:
                            wge(eng, ls, xs[jj - 1])
                            av_section(jj - 1)
                    wge(eng, ls, xs[3])
                    av_section(3)
                    for jj in range(4):
                        eng.reg_add(vs[jj], vs[jj], 16)
                        eng.reg_add(xs[jj], xs[jj], 16)
                    eng.reg_add(wd, wd, 48)
                    eng.reg_sub(n, n, 1)
                close_groups()
                eng.reg_mul(tmp, nb0, 48)
                eng.reg_add(bld, bld, tmp)
                eng.reg_mul(tmp, nb0, 16)
                eng.reg_add(bls, bls, tmp)
                return eng.matmul(pS[0][0:1, 0:1], zeros_bf[0:1, 0:1],
                                  p_sb[0:1, 0:1], start=True, stop=True)

            o_T1 = A("tensor", f_tens, deps=[o_clear], tag=f"qkav{s}_{h}_{gp}")

            def f_vec(eng):
                if _STRIP >= 6:
                    eng.memset(linv[0:1, 0:512], 1.0)
                    return eng.memset(linv[0:1, 512:1024], 1.0)
                (n, nb0, blt, bld, el, tmp, q0r, q1r, q2r, q3r, dw) = getregs(
                    eng, "vector", ["n", "nb0", "blt", "bld", "el", "tmp",
                                    "q0", "q1", "q2", "q3", "dw"])
                wq = (q0r, q1r, q2r, q3r)
                if s == 0 and h == 0 and gp == 0:
                    eng.reg_mov(blt, 0)
                    eng.reg_mov(bld, 0)
                if slot_start:
                    eng.reg_add(bld, bld, 16)
                eng.reg_load(n, scal_sb[0:1, s:s + 1])
                eng.reg_add(nb0, n, 0)
                for jj in range(4):
                    eng.reg_add(wq[jj], blt, 4 + (4, 8, 20, 32)[jj])
                eng.reg_add(dw, bld, 48)
                eng.reg_mul(el, nb0, 48)
                eng.reg_add(el, el, 8)
                eng.reg_add(el, el, blt)
                if _STRIP >= 5:
                    eng.memset(linv[0:1, 0:512], 1.0)
                    return eng.memset(linv[0:1, 512:1024], 1.0)
                with eng.While(n):
                    wge(eng, ld, dw)
                    for jj in range(4):
                        wge(eng, lt, wq[jj])
                        for j in (2 * jj, 2 * jj + 1):
                            for g in range(2):
                                if _STRIP >= 1:
                                    eng.memset(linv[0:1, 1:2], 0.0).then_inc(lv, 1)
                                else:
                                    eng.scalar_tensor_tensor(
                                        pS[(2 * j + g) % 4][:, :],
                                        d_sb[:, (7 - j) * 512:(8 - j) * 512],
                                        -slopes[g],
                                        pS[(2 * j + g) % 4][:, :],
                                        ALU.mult, ALU.add).then_inc(lv, 1)
                    for jj in range(4):
                        eng.reg_add(wq[jj], wq[jj], 48)
                    eng.reg_add(dw, dw, 48)
                    eng.reg_sub(n, n, 1)
                wge(eng, lt, el)
                eng.reg_mul(tmp, nb0, 48)
                eng.reg_add(blt, blt, tmp)
                eng.reg_add(blt, blt, 8)
                eng.reg_mul(tmp, nb0, 48)
                eng.reg_add(bld, bld, tmp)
                if _STRIP >= 1:
                    eng.memset(linv[0:1, 0:512], 1.0)
                    return eng.memset(linv[0:1, 512:1024], 1.0)
                eng.reciprocal(linv[0:1, 0:512], pL[0][0:1, :])
                return eng.reciprocal(linv[0:1, 512:1024], pL[1][0:1, :])

            o_vec = A("vector", f_vec, deps=[o_clear], tag=f"stt{s}_{h}_{gp}")

            def f_scal(eng):
                if _STRIP >= 6:
                    return eng.activation(stats[:, 55:56], zero_t[:, :], AF.Copy,
                                          bias=0.0, scale=1.0)
                (n, nb0, blv, blt, wpt, tmp, l0, l1, l2, l3, p2r, p3r) = getregs(
                    eng, "scalar", ["n", "nb0", "blv", "blt", "wpt", "tmp",
                                    "l0", "l1", "l2", "l3", "p2", "p3"])
                wl = (l0, l1, l2, l3)
                wpp = {2: p2r, 3: p3r}
                if s == 0 and h == 0 and gp == 0:
                    eng.reg_mov(blv, 0)
                    eng.reg_mov(blt, 0)
                eng.reg_load(n, scal_sb[0:1, s:s + 1])
                eng.reg_add(nb0, n, 0)
                eng.reg_add(wpt, blt, 4)
                for jj in range(4):
                    eng.reg_add(wl[jj], blv, 4 * (jj + 1))
                for jj in (2, 3):
                    eng.reg_add(wpp[jj], blt, 4 + (16, 28)[jj - 2])
                if _STRIP >= 5:
                    return eng.activation(stats[:, 55:56], zero_t[:, :], AF.Copy,
                                          bias=0.0, scale=1.0)
                with eng.While(n):
                    wge(eng, lt, wpt)
                    for jj in range(4):
                        wge(eng, lv, wl[jj])
                        if jj >= 2:
                            wge(eng, lt, wpp[jj])
                        for j in (2 * jj, 2 * jj + 1):
                            for g in range(2):
                                sl = (j % 4) * 2 + g
                                if _STRIP >= 2:
                                    eng.activation(stats[:, 55:56], zero_t[:, :], AF.Copy,
                                                   bias=0.0, scale=1.0).then_inc(ls, 1)
                                elif _KEXP == 1:
                                    eng.activation(
                                        p_sb[:, sl * 512:(sl + 1) * 512],
                                        pS[(2 * j + g) % 4][:, :],
                                        AF.Copy, bias=0.0, scale=1.0).then_inc(ls, 1)
                                elif _KEXP == 2:
                                    eng.activation(stats[:, 55:56], zero_t[:, :], AF.Exp,
                                                   bias=zero_t[:, :], scale=1.0).then_inc(ls, 1)
                                elif _KEXP == 3:
                                    eng.activation(
                                        lbc[:, 0:512], pS[(2 * j + g) % 4][:, :],
                                        AF.Copy, bias=0.0, scale=1.0).then_inc(ls, 1)
                                elif _KEXP == 4:
                                    eng.activation(
                                        p_sb[:, sl * 512:(sl + 1) * 512],
                                        d_sb[:, 0:512],
                                        AF.Copy, bias=0.0, scale=1.0).then_inc(ls, 1)
                                elif _KEXP == 5:
                                    eng.activation(
                                        lbc[:, 0:512], d_sb[:, 0:512],
                                        AF.Copy, bias=0.0, scale=1.0).then_inc(ls, 1)
                                elif _KEXP == 6:
                                    eng.activation(
                                        lbc[:, 0:512], t2b[:, 0:512],
                                        AF.Copy, bias=0.0, scale=1.0).then_inc(ls, 1)
                                else:
                                    eng.activation(
                                        p_sb[:, sl * 512:(sl + 1) * 512],
                                        pS[(2 * j + g) % 4][:, :],
                                        AF.Exp, bias=zero_t[:, :], scale=1.0).then_inc(ls, 1)
                    for jj in range(4):
                        eng.reg_add(wl[jj], wl[jj], 16)
                    for jj in (2, 3):
                        eng.reg_add(wpp[jj], wpp[jj], 48)
                    eng.reg_add(wpt, wpt, 48)
                    eng.reg_sub(n, n, 1)
                eng.reg_mul(tmp, nb0, 16)
                eng.reg_add(blv, blv, tmp)
                eng.reg_mul(tmp, nb0, 48)
                eng.reg_add(blt, blt, tmp)
                eng.reg_add(blt, blt, 8)
                return eng.activation(stats[:, 55:56], zero_t[:, :], AF.Copy,
                                      bias=0.0, scale=1.0)

            o_scal = A("scalar", f_scal, deps=[o_clear], tag=f"exp{s}_{h}_{gp}")

            def f_T2(eng):
                eng.matmul(pS[0][:, :], ones_f[:, :], linv[0:1, 0:512], start=True, stop=True)
                return eng.matmul(pS[1][:, :], ones_f[:, :], linv[0:1, 512:1024],
                                  start=True, stop=True)

            o_T2 = A("tensor", f_T2, deps=[o_vec, o_scal, o_onf], tag=f"bc{s}_{h}_{gp}")
            o_sep = A("scalar", lambda e: e.activation(
                lbc[:, 0:512], pS[0][:, :], AF.Copy, bias=0.0, scale=1.0),
                deps=[o_T2], tag=f"lbc{s}_{h}_{gp}")
            o_ep2a = A("vector", lambda e: e.tensor_tensor(
                big_fm[:, heads[0] * 512:(heads[0] + 1) * 512],
                lbc[:, 0:512] if _STRIP >= 6 else pO[0][:, :],
                lbc[:, 0:512], ALU.mult),
                deps=[o_sep, o_T1], tag=f"ao{s}_{h}_{gp}")
            o_sep2 = A("scalar", lambda e: e.activation(
                lbc[:, 0:512], pS[1][:, :], AF.Copy, bias=0.0, scale=1.0),
                deps=[o_T2, o_ep2a], tag=f"lbc2{s}_{h}_{gp}")
            o_ep2 = A("vector", lambda e: e.tensor_tensor(
                big_fm[:, heads[1] * 512:(heads[1] + 1) * 512],
                lbc[:, 0:512] if _STRIP >= 6 else pO[1][:, :],
                lbc[:, 0:512], ALU.mult),
                deps=[o_sep2], tag=f"ao2{s}_{h}_{gp}")

            for e2, o2 in (("gpsimd", o_clear), ("scalar", o_sep2),
                           ("vector", o_ep2), ("tensor", o_T2)):
                prev_pass[e2] = o2
            return o_T1, o_ep2

        def fm512(ci, w=512):
            return big_fm[:, ci * 512: ci * 512 + w]

        def fm512_group(g4, col=0):
            r = big_fm.rearrange("p (c t) -> p c t", c=32)
            return r[:, g4:g4 + 4, col:col + 128]

        chain_state = {"cast": [None, None], "lastmm": [None, None]}

        def stream_wB(idx, src_ap, tag, extra_dep=None):
            p = idx % 2
            d = A(GPD, lambda e, p=p, ap=src_ap: e.dma_start(out=slabB_dst(p), in_=ap),
                  deps=[chain_state["cast"][p], extra_dep], dma=True, tag=f"wd_{tag}")
            c = A("gpsimd", lambda e, p=p: e.tensor_copy(wbf[p][:, :], wsl[p][:, :]),
                  deps=[d, chain_state["lastmm"][p]], tag=f"wc_{tag}")
            chain_state["cast"][p] = c
            return c

        def slabB_dst(p):
            return wsl[p][:, :].rearrange("p (c o) -> p c o", c=32)

        def col_slabB(src2d, col0):
            return src2d.rearrange("(c p) o -> p c o", p=128)[:, :, col0:col0 + 128]

        def chain_slot(s, deps_attn):
            t0 = s * 512
            prev_evac = [None, None]
            prev_tp_consumer = [deps_attn[-1], deps_attn[-1]]
            prev_xdma = [None, None]
            prev_x2dma = [None, None]
            for oc in range(32):
                p = oc % 2
                c = stream_wB(oc, col_slabB(wo, oc * 128), f"o{s}_{oc}",
                              extra_dep=(deps_attn[0] if oc == 0 else None))
                dd = [c, prev_evac[p]] + (deps_attn if oc < 2 else [])
                o_mm = None
                for ci in range(32):
                    o_mm = A("tensor", lambda e, p=p, ci=ci: e.matmul(
                        pO[p][:, :], wbf[p][:, ci * 128:(ci + 1) * 128], fm512(ci),
                        start=(ci == 0), stop=(ci == 31)), deps=dd if ci == 0 else [],
                        tag=f"omm{s}_{oc}_{ci}")
                chain_state["lastmm"][p] = o_mm
                o_ev = A("scalar", lambda e, p=p: e.activation(
                    fev[p][:, :], pO[p][:, :], AF.Copy, bias=0.0, scale=1.0),
                    deps=[o_mm], tag=f"oev{s}_{oc}")
                prev_evac[p] = o_ev
                d_x1 = A(GPD, lambda e, p=p, oc=oc, t0=t0: e.dma_start(
                    out=xc[p][:, :].rearrange("q (tt o) -> q tt o", tt=4),
                    in_=x1.rearrange("(tt q) o -> q tt o", q=128)[:, 4 * s:4 * s + 4,
                                                                 oc * 128:(oc + 1) * 128]),
                    deps=[prev_xdma[p]], dma=True, tag=f"x1rd{s}_{oc}")
                prev_xdma[p] = d_x1
                bank = pS[2 + (oc % 2)]
                o_tp = None
                for u in range(4):
                    ddt = []
                    if u == 0:
                        ddt = [o_ev, prev_tp_consumer[oc % 2]]
                    o_tp = A("tensor", lambda e, bank=bank, u=u, p=p: e.transpose(
                        bank[:, u * 128:(u + 1) * 128], fev[p][:, u * 128:(u + 1) * 128],
                        id_f[:, :]), deps=ddt, tag=f"otp{s}_{oc}_{u}")
                o_add = A("vector", lambda e, bank=bank, p=p: e.tensor_tensor(
                    x2c[:, :], bank[:, :], xc[p][:, :], ALU.add),
                    deps=[o_tp, d_x1, prev_x2dma[p]], tag=f"x2add{s}_{oc}")
                prev_tp_consumer[oc % 2] = o_add
                prev_x2dma[p] = A(GPD, lambda e, oc=oc, s=s: e.dma_start(
                    out=x2d.rearrange("(tt q) o -> q tt o", q=128)[:, 4 * s:4 * s + 4,
                                                                  oc * 128:(oc + 1) * 128],
                    in_=x2c[:, :].rearrange("q (tt o) -> q tt o", tt=4)),
                    deps=[o_add], dma=True, tag=f"x2w{s}_{oc}")

            bar_c1 = sch.barrier()

            prev_norm = [None, None]
            prev_lastT = [None, None]
            prev_pmf = None
            prev_lgt = None
            prev_werev = None
            prev_lg = None
            for it in range(4):
                p = it % 2
                t = xt[0]
                tcol = it * 128
                d_x = A(GPD, lambda e, it=it, s=s: e.dma_start(
                    out=xt[0][:, :], in_=x2d[t0 + it * 128:t0 + (it + 1) * 128, :]),
                    deps=[prev_norm[0], prev_lastT[0], bar_c1["gpsimd"],
                          prev_x2dma[0], prev_x2dma[1]], dma=True, tag=f"x2rd{s}_{it}")
                o_bn = None
                for j in range(8):
                    o_bn = A("vector", lambda e, j=j: e.bn_stats(
                        stats[:, 6 * j:6 * j + 6], xt[0][:, 512 * j:512 * (j + 1)]),
                        deps=[d_x, bar_c1["vector"]] if j == 0 else [], tag=f"bn3{s}{it}{j}")
                o_ag = A("vector", lambda e: e.bn_aggr(stats[:, 48:50], stats[:, 0:48]),
                         deps=[o_bn], tag=f"ag3{s}{it}")
                o_ln = A("scalar", lambda e: e.activation(
                    stats[:, 50:51], stats[:, 49:50], AF.Ln, bias=eps_t[:, :], scale=1.0),
                    deps=[o_ag], tag=f"l3{s}{it}")
                o_rstd = A("scalar", lambda e: e.activation(
                    stats[:, 51:52], stats[:, 50:51], AF.Exp, bias=zero_t[:, :], scale=-0.5),
                    deps=[o_ln], tag=f"r3{s}{it}")
                o_nm = A("vector", lambda e: e.tensor_scalar(
                    xt[0][:, :], xt[0][:, :], stats[:, 48:49], stats[:, 51:52],
                    ALU.subtract, ALU.mult), deps=[o_rstd], tag=f"n3{s}{it}")
                prev_norm[0] = o_nm
                ev_b = [None] * 8
                ev_f32 = [None] * 8
                router_mm = [None] * 8
                for g in range(8):
                    bank = pS[g % 2]
                    deps_first = [o_nm]
                    if g >= 2:
                        deps_first.append(ev_b[g - 2])
                    else:
                        deps_first.append(prev_pmf if g == 1 else prev_lgt)
                        if it == 0:
                            deps_first.append(bar_c1["scalar"])
                            deps_first.append(bar_c1["tensor"])
                    o_tp = None
                    for u in range(4):
                        ci = g * 4 + u
                        o_tp = A("tensor", lambda e, bank=bank, u=u, ci=ci: e.transpose(
                            bank[:, u * 128:(u + 1) * 128], xt[0][:, ci * 128:(ci + 1) * 128],
                            id_f[:, :]), deps=deps_first if u == 0 else [], tag=f"t3{s}{it}{ci}")
                    prev_lastT[0] = o_tp
                    xl_half = 0
                    dep_xlf = [router_mm[g - 1]] if g >= 1 else []
                    ev_f32[g] = A("scalar", lambda e, bank=bank, xl_half=xl_half: e.activation(
                        xlf[:, xl_half:xl_half + 512], bank[:, :], AF.Copy, bias=0.0, scale=1.0),
                        deps=[o_tp] + dep_xlf, tag=f"ef3{s}{it}{g}")
                    ev_b[g] = A("scalar", lambda e, bank=bank, g=g, tcol=tcol: e.activation(
                        fm512_group(g * 4, tcol), bank[:, :], AF.Copy, bias=0.0, scale=1.0),
                        deps=[o_tp], tag=f"eb3{s}{it}{g}")
                    o_mm = None
                    for u in range(4):
                        ci = g * 4 + u
                        ddr = [ev_f32[g], d_wr]
                        if g == 0 and u == 0 and prev_lg is not None:
                            ddr.append(prev_lg)
                        o_mm = A("tensor", lambda e, ci=ci, u=u, xl_half=xl_half: e.matmul(
                            pS[2][0:NE, 0:128], wr_sb[:, ci * NE:(ci + 1) * NE],
                            xlf[:, xl_half + u * 128:xl_half + (u + 1) * 128],
                            start=(ci == 0), stop=(ci == 31)),
                            deps=ddr if u == 0 else [], tag=f"rm3{s}{it}{ci}")
                    router_mm[g] = o_mm
                o_lg = A("scalar", lambda e: e.activation(
                    lg_fm[:, :], pS[2][0:NE, 0:128], AF.Identity, bias=rb_sb[:, :], scale=1.0),
                    deps=[router_mm[7], d_rb], tag=f"lg3{s}{it}")
                prev_lg = o_lg
                o_tp15 = A("tensor", lambda e: e.transpose(
                    pS[3][:, 0:NE], lg_fm[:, :], id_f[0:NE, 0:NE]),
                    deps=[o_lg, ev_b[6]], tag=f"tq3{s}{it}")
                o_lgt = A("scalar", lambda e: e.activation(
                    lg_tm[:, :], pS[3][:, 0:NE], AF.Copy, bias=0.0, scale=1.0),
                    deps=[o_tp15], tag=f"lt3{s}{it}")
                prev_lgt = o_lgt
                o_mx = A("vector", lambda e: e.reduce_max(
                    stats[:, 52:53], lg_tm[:, :], axis=AX.X), deps=[o_lgt], tag=f"m3{s}{it}")
                o_ng = A("vector", lambda e: e.tensor_scalar(
                    stats[:, 52:53], stats[:, 52:53], -1.0, None, ALU.mult), deps=[o_mx])
                o_ex = A("scalar", lambda e: e.activation(
                    prb[:, :], lg_tm[:, :], AF.Exp, bias=stats[:, 52:53], scale=1.0,
                    accum_out=stats[:, 53:54]), deps=[o_ng], tag=f"sx3{s}{it}")
                o_rcp = A("vector", lambda e: e.reciprocal(stats[:, 54:55], stats[:, 53:54]),
                          deps=[o_ex])
                o_cn = None
                for j in range(NE):
                    o_cn = A("vector", lambda e, j=j: e.tensor_scalar(
                        scr15[:, :], prb[:, :], prb[:, j:j + 1], 0.0, ALU.is_gt, ALU.add,
                        accum_out=cnt[:, j:j + 1]),
                        deps=[o_ex] if j == 0 else [o_cn], tag=f"c3{s}{it}{j}")
                o_mk = A("vector", lambda e: e.tensor_scalar(
                    msk[:, :], cnt[:, :], 7.5, None, ALU.is_lt), deps=[o_cn])
                o_pm = A("vector", lambda e: e.scalar_tensor_tensor(
                    pm[:, :], prb[:, :], stats[:, 54:55], msk[:, :], ALU.mult, ALU.mult),
                    deps=[o_mk, o_rcp], tag=f"pm3{s}{it}")
                o_tpm = A("tensor", lambda e: e.transpose(
                    pS[3][0:NE, 0:128], pm[:, :], id_f[:, :]), deps=[o_pm, o_lgt, ev_b[7]],
                    tag=f"tm3{s}{it}")
                o_pmf = A("scalar", lambda e: e.activation(
                    pm_fm[:, :], pS[3][0:NE, 0:128], AF.Copy, bias=0.0, scale=1.0),
                    deps=[o_tpm], tag=f"pf3{s}{it}")
                prev_pmf = o_pmf
                ddw = [o_pmf, o_ex2]
                if prev_werev is not None:
                    ddw.append(prev_werev)
                o_wmm = A("tensor", lambda e: e.matmul(
                    pS[2][:, 0:128], ex2[:, :], pm_fm[:, :], start=True, stop=True),
                    deps=ddw, tag=f"wm3{s}{it}")
                o_werev = A("scalar", lambda e, t0=t0, tcol=tcol: e.activation(
                    w_er[:, tcol:tcol + 128], pS[2][:, 0:128], AF.Copy,
                    bias=0.0, scale=2.0), deps=[o_wmm], tag=f"we3{s}{it}")
                prev_werev = o_werev

            bar_c2 = sch.barrier()

            o_hh = None
            for ci in range(32):
                o_hh = A("tensor", lambda e, ci=ci: e.matmul(
                    pS[0][:, :], at_bf[:, ci * 128:(ci + 1) * 128], fm512(ci),
                    start=(ci == 0), stop=(ci == 31)),
                    deps=[bar_c2["scalar"], o_atb] if ci == 0 else [], tag=f"h2{s}_{ci}")
            o_hp = A("vector", lambda e, t0=t0: e.tensor_tensor(
                hpr[:, :], pS[0][:, :], w_er[:, 0:512], ALU.mult),
                deps=[o_hh, bar_c2["scalar"]], tag=f"hp2{s}")
            prev_evac2 = [None, None]
            prev_tpc = [None, None]
            prev_xd = [None, None]
            prev_ydma = [None, None]
            for oc in range(32):
                p = oc % 2
                c = stream_wB(64 + 32 * s + oc, col_slabB(wb_t, oc * 128), f"b2_{s}_{oc}")
                dd = [c, o_hp, prev_evac2[p]]
                o_mm = None
                for ci in range(32):
                    o_mm = A("tensor", lambda e, p=p, ci=ci: e.matmul(
                        pO[p][:, :], wbf[p][:, ci * 128:(ci + 1) * 128], fm512(ci),
                        start=(ci == 0), stop=False), deps=dd if ci == 0 else [],
                        tag=f"fm2{s}_{oc}_{ci}")
                o_mm = A("tensor", lambda e, p=p, oc=oc: e.matmul(
                    pO[p][:, :], bt_bf[:, oc * 128:(oc + 1) * 128], hpr[:, :],
                    start=False, stop=True), deps=[o_btb] if oc == 0 else [],
                    tag=f"lo2{s}_{oc}")
                chain_state["lastmm"][p] = o_mm
                o_ev = A("scalar", lambda e, p=p, oc=oc: e.activation(
                    fev[p][:, :], pO[p][:, :], AF.Identity, bias=bbh[:, oc:oc + 1], scale=0.5),
                    deps=[o_mm, o_bbh], tag=f"fe2{s}_{oc}")
                prev_evac2[p] = o_ev
                d_x2 = A(GPD, lambda e, p=p, oc=oc, s=s: e.dma_start(
                    out=xc[p][:, :].rearrange("q (tt o) -> q tt o", tt=4),
                    in_=x2d.rearrange("(tt q) o -> q tt o", q=128)[:, 4 * s:4 * s + 4,
                                                                  oc * 128:(oc + 1) * 128]),
                    deps=[prev_xd[p]], dma=True, tag=f"x2r2{s}_{oc}")
                prev_xd[p] = d_x2
                bank = pS[2 + (oc % 2)]
                o_tp = None
                for u in range(4):
                    ddt = []
                    if u == 0:
                        ddt = [o_ev, prev_tpc[oc % 2]]
                        if oc < 2:
                            ddt.append(bar_c2["vector"])
                    o_tp = A("tensor", lambda e, bank=bank, u=u, p=p: e.transpose(
                        bank[:, u * 128:(u + 1) * 128], fev[p][:, u * 128:(u + 1) * 128],
                        id_f[:, :]), deps=ddt, tag=f"yt{s}_{oc}_{u}")
                o_add = A("vector", lambda e, bank=bank, p=p: e.tensor_tensor(
                    x2c[:, :], bank[:, :], xc[p][:, :], ALU.add),
                    deps=[o_tp, d_x2, prev_ydma[p]], tag=f"yadd{s}_{oc}")
                prev_tpc[oc % 2] = o_add
                prev_ydma[p] = A(GPD, lambda e, oc=oc, s=s: e.dma_start(
                    out=y.rearrange("(tt q) o -> q tt o", q=128)[:, 4 * s:4 * s + 4,
                                                                 oc * 128:(oc + 1) * 128],
                    in_=x2c[:, :].rearrange("q (tt o) -> q tt o", tt=4)),
                    deps=[o_add], dma=True, tag=f"ydma{s}_{oc}")
            return [prev_ydma[0], prev_ydma[1]]

        final_y = []
        for s in range(nslot_b):
            deps_attn = []
            for h in range(NKV):
                for gp in range(2):
                    if h * 2 + gp >= npass:
                        continue
                    t1, ep2 = attn_pass(s, h, gp, deps_attn[-1:] if gp == 99 else [])
                    deps_attn.append(ep2)
            if attn_only:
                dump_eng = GPD if _KDUMP in (1, 3) else "sync"
                if _KDUMP in (1, 2):
                    yb = y.bitcast(BF16).rearrange("(a p) o -> a p o", p=128)
                    for a_ in range(2):
                        final_y.append(A(dump_eng, lambda e, s=s, a_=a_: e.dma_start(
                            out=yb[2 * s + a_],
                            in_=big_fm[:, a_ * 8192:(a_ + 1) * 8192]),
                            deps=deps_attn, dma=True, tag=f"dump{s}_{a_}"))
                else:
                    yf = y.rearrange("(a p) o -> a p o", p=128)
                    for a_ in range(2):
                        final_y.append(A(dump_eng, lambda e, s=s, a_=a_: e.dma_start(
                            out=yf[2 * s + a_], in_=d_sb[:, :]),
                            deps=deps_attn, dma=True, tag=f"dump{s}_{a_}"))
            else:
                final_y += chain_slot(s, deps_attn)
            b = sch.barrier()
            for e2 in ("tensor", "vector", "scalar", "gpsimd"):
                prev_pass[e2] = b[e2]

        o_tail = A("gpsimd", lambda e: e.memset(scr15[0:1, 0:1], 0.0),
                   deps=final_y, tag="tail")

        sch.emit(st.enter_context(nc.Block()), sems, sim_strict=sim_strict)
    return nc



def _prep_launchA_inputs(inputs):
    x = np.ascontiguousarray(inputs["L0_attn_hidden_states"].reshape(S, D))
    wb_t = np.ascontiguousarray(inputs["L0_ffn_base_W"].T)
    wr_t = np.ascontiguousarray(inputs["L0_ffn_router_W"].T)
    a_t = np.ascontiguousarray(inputs["L0_ffn_A"].transpose(2, 0, 1).reshape(D, ER))
    b_t = np.ascontiguousarray(inputs["L0_ffn_B"].transpose(0, 2, 1).reshape(ER, D))
    common = {
        "wb_t": wb_t, "bb": np.ascontiguousarray(inputs["L0_ffn_base_b"]),
        "wr_t": wr_t, "rb": np.ascontiguousarray(inputs["L0_ffn_router_b"]),
        "a_t": a_t, "b_t": b_t,
        "emap": np.ascontiguousarray(inputs["L0_ffn_expert_map"]),
        "wqkv": np.ascontiguousarray(inputs["L0_attn_w_qkv"]),
    }
    maps = []
    for c in range(NCORE):
        m = dict(common)
        m["x0"] = np.ascontiguousarray(x[c * TA:(c + 1) * TA])
        maps.append(m)
    return maps


def _prep_launchB_maps(inputs, resA):
    kg = np.concatenate([resA[c]["k_out"] for c in range(NCORE)], axis=0)
    vg = np.concatenate([resA[c]["v_out"] for c in range(NCORE)], axis=0)
    common = {
        "kg": np.ascontiguousarray(kg), "vg": np.ascontiguousarray(vg),
        "wo": np.ascontiguousarray(inputs["L0_attn_w_o"]),
        "wb_t": np.ascontiguousarray(inputs["L0_ffn_base_W"].T),
        "bb": np.ascontiguousarray(inputs["L0_ffn_base_b"]),
        "wr_t": np.ascontiguousarray(inputs["L0_ffn_router_W"].T),
        "rb": np.ascontiguousarray(inputs["L0_ffn_router_b"]),
        "a_t": np.ascontiguousarray(
            inputs["L0_ffn_A"].transpose(2, 0, 1).reshape(D, ER)),
        "b_t": np.ascontiguousarray(
            inputs["L0_ffn_B"].transpose(0, 2, 1).reshape(ER, D)),
        "emap": np.ascontiguousarray(inputs["L0_ffn_expert_map"]),
    }
    maps = []
    for c in range(NCORE):
        scal = np.zeros(8, np.int32)
        scal[0] = c + 1
        scal[1] = c + 1
        scal[2] = (c * TA + TW // 2) // 128 - 7
        scal[3] = (c * TA + 512 + TW // 2) // 128 - 7
        m = dict(common)
        m["x1"] = np.ascontiguousarray(resA[c]["x1_out"])
        m["q_in"] = np.ascontiguousarray(resA[c]["q_out"])
        m["scal"] = scal
        maps.append(m)
    return maps


def kernel(**inputs):
    from concourse.bass_utils import run_bass_kernel_spmd
    mapsA = _prep_launchA_inputs(inputs)
    ncA = build_launchA()
    rA = run_bass_kernel_spmd(ncA, mapsA, core_ids=list(range(NCORE)))
    mapsB = _prep_launchB_maps(inputs, rA.results)
    ncB = build_launchB()
    rB = run_bass_kernel_spmd(ncB, mapsB, core_ids=list(range(NCORE)))
    y = np.concatenate([rB.results[c]["y"] for c in range(NCORE)], axis=0)
    return np.ascontiguousarray(y.reshape(1, S, D)).astype(np.float32)
